# version 06 (final)
# baseline (speedup 1.0000x reference)
import contextlib
import numpy as np
import concourse.bass as bass
import concourse.mybir as mybir
from concourse.bass_utils import run_bass_kernel_spmd

F32 = mybir.dt.float32
AF = mybir.ActivationFunctionType
OP = mybir.AluOpType
AX = mybir.AxisListType

T = 2048
NCH = 8
DM = 1024
NE = 8
HMOE = 2736
NHT = 22
HT_SIZES = [128] * 21 + [48]
HT_OFF = [128 * i for i in range(22)]
PI = float(np.pi)
RG = [[0, 1], [2, 3], [4, 5], [6, 7]]
DEBUG_TAPS = ()
PHASE_LIMIT = 99
SCAN_WAVE_DUMP = False


class Phase:
    def __init__(self, nc, name, ctx=None):
        self.nc, self.name = nc, name
        self.sems = {}

    def sem(self, tag):
        if tag not in self.sems:
            self.sems[tag] = self.nc.alloc_semaphore(name=f"{self.name}_{tag}")
        return self.sems[tag]

    def tick(self, inst, e):
        self.nt = getattr(self, "nt", 0) + 1
        inst.then_inc(self.sem("_t"), 1)
        e.wait_ge(self.sem("_t"), self.nt)

    def close(self):
        with self.nc.Block(self.name + "_c") as bl:
            @bl.gpsimd
            def _(g):
                self.nc.clear_and_free_semaphores(list(self.sems.values()))


class Builder:
    def __init__(self):
        self.nc = bass.Bass("TRN2", target_bir_lowering=False,
                            detect_race_conditions=False)
        self.dram, self.sb = {}, {}
        self.ctx = contextlib.ExitStack()
        self.phases = {}

    def din(self, name, shape):
        self.dram[name] = self.nc.dram_tensor(name, list(shape), F32,
                                              kind="ExternalInput").ap()

    def dout(self, name, shape):
        self.dram[name] = self.nc.dram_tensor(name, list(shape), F32,
                                              kind="ExternalOutput").ap()

    def dint(self, name, shape):
        self.dram[name] = self.nc.dram_tensor(name, list(shape), F32,
                                              kind="Internal").ap()

    def sbuf(self, name, shape, dtype=F32):
        self.sb[name] = self.nc.alloc_sbuf_tensor("s_" + name, list(shape),
                                                  dtype).ap()

    def declare_io(self):
        d = self.din
        d("x0T", [DM, T])
        for n in ("nu_log_pm", "theta_log_pm", "gamma_log_pm", "D_pm",
                  "f_bo_pm", "f_g1_pm", "f_g2_pm"):
            d(n, [128, NCH])
        d("B_reT", [DM, DM]); d("B_imT", [DM, DM])
        d("C_reT", [DM, DM]); d("C_imT", [DM, DM])
        d("centroidT", [DM, NE])
        d("moe_Wg", [NE, DM, HMOE]); d("moe_W1", [NE, DM, HMOE])
        d("moe_W2", [NE, HMOE, DM])
        d("moe_bg_pm", [NE, 128, NHT]); d("moe_b1_pm", [NE, 128, NHT])
        d("moe_b2_pm", [NE, 128, NCH])
        d("attn_kernel", [3, DM, DM])
        d("f_Wq", [DM, DM]); d("f_bq", [1, DM])
        d("f_Wdown", [DM, 64]); d("f_bdown", [64, 1])
        d("f_Wup", [65, 2 * DM])
        d("f_Wo", [DM, DM])
        d("f_WgE", [DM, NE]); d("f_bgE", [1, NE])
        d("f_We", [NE, DM, DM])
        d("identity", [128, 128])
        d("cmask", [128, 1])
        self.dout("outT", [DM, T])
        self.dint("cc_carry_in", [128, 4])
        self.dint("cc_carry_out", [2 * 128, 4])
        self.dint("cc_K_in", [DM, T])
        self.dint("cc_K_out", [2 * DM, T])
        self.dint("cc_V_in", [T, DM])
        self.dint("cc_V_out", [2 * T, DM])
        self.dint("we_dram", [NE, T])
        if SCAN_WAVE_DUMP:
            self.dout("dbg_xs_waves", [128, 4 * 8192])
            self.dout("dbg_small", [128, 256])
        self.dint("gate_dram", [NE, 512])

    def declare_sbuf(self):
        s = self.sbuf
        s("x", [128, NCH * T])
        s("xn", [128, NCH * T])
        s("scr", [128, 11264])
        s("wbuf", [128, 4096])
        s("p0r", [1, 1024])
        s("bc_row", [1, 512])
        s("ones_col", [1, 128])
        s("ones_part", [128, 1])
        s("identity", [128, 128])
        for n in ("nu_log_pm", "theta_log_pm", "gamma_log_pm", "D_pm",
                  "f_bo_pm", "f_g1_pm", "f_g2_pm"):
            s(n, [128, NCH])
        s("cmask", [128, 1])
        s("cm4", [128, 1])
        s("lam_re", [128, NCH]); s("lam_im", [128, NCH])
        s("gamma", [128, NCH])
        s("lp_re", [128, NCH * 12]); s("lp_im", [128, NCH * 12])
        s("lp_im_neg", [128, NCH * 12])
        s("param8", [128, 96])
        s("centroidT", [128, NCH * NE])
        s("c2b", [128, NE])
        s("xn2_tm", [128, 16])
        s("wrow", [128, 16])
        s("att_mx", [128, 16]); s("att_s", [128, 16])
        s("carry", [128, 4]); s("carry_in", [128, 8])
        s("c0", [128, 4]); s("c0n", [128, 4])
        s("cj_re", [128, 1]); s("cj_im", [128, 1]); s("cj_imn", [128, 1])
        s("cj2_re", [128, 1]); s("cj2_im", [128, 1])
        s("b2_t", [128, NE])
        s("bias_moe", [128, 2 * NHT])
        s("ev_a", [128, 512]); s("ev_b", [128, 512])
        s("wdown_sb", [128, 512])
        s("bdown_sb", [64, 1])
        s("wgE_sb", [128, NE * NCH])
        s("gate_tm", [128, 64])

    def psum_banks(self):
        self.pb = [self.nc.alloc_psum_tensor(f"pb{i}", [128, 1024], F32).ap()
                   for i in range(4)]

    def mm(self, e, out, lhsT, rhs, start, stop):
        n = rhs.shape[-1]
        last = None
        for o in range(0, n, 512):
            w = min(512, n - o)
            last = e.matmul(out[:, o:o + w], lhsT, rhs[:, o:o + w],
                            start=start, stop=stop)
        return last

    def xsl(self, name, c, t0, w):
        return self.sb[name][:, c * T + t0:c * T + t0 + w]

    def build(self):
        self.declare_io()
        self.declare_sbuf()
        self.psum_banks()
        rem = self.nc.sbuf_bytes_remaining
        assert rem >= 0, f"SBUF over budget by {-rem} bytes/partition"
        L = PHASE_LIMIT
        self.phase0_load()
        if L >= 1:
            self.phase1_ln(0)
            self._tap("xn0", "xn")
        if L >= 2:
            self.phase2_routing()
        if L >= 3:
            self.phase3_scan()
            self._tap("x_scan", "x")
            self._tap("xs_w3", "scr")
        if L >= 4:
            self.phase4_moe()
            self._tap("x_moe", "x")
        if L >= 5:
            self.phase1_ln(1)
            self._tap("xn1", "xn")
        if L >= 6:
            self.phase6_kv()
        if L >= 7:
            self.phase7_attn()
            self._tap("x_att", "x")
        if L >= 8:
            self.phase8_ffnA()
            self._tap("xmid", "xn")
            self._tap("x_ffA", "x")
        if L >= 9:
            self.phase9_ffnB()
        self.phase10_store()
        return self.nc

    def _tap(self, name, buf):
        if name not in DEBUG_TAPS:
            return
        nc = self.nc
        ncols = self.sb[buf].shape[-1]
        self.dout("dbg_" + name, [128, ncols])
        dst = self.dram["dbg_" + name]
        with nc.Block("tap_" + name) as bl:
            p = Phase(nc, "tap" + name)

            @bl.sync
            def _(e):
                e.dma_start(out=dst[:], in_=self.sb[buf][:]
                            ).then_inc(p.sem("d"), 16)
                e.wait_ge(p.sem("d"), 16)
        p.close()

    def phase0_load(self):
        nc, sb, dr = self.nc, self.sb, self.dram
        with nc.Block("p0") as bl:
            p = Phase(nc, "p0")
            self.phases["p0"] = p
            nld = [0]

            @bl.sync
            def _(e):
                def dma(out, in_):
                    e.dma_start(out=out, in_=in_).then_inc(p.sem("ld"), 16)
                    nld[0] += 16
                for c in range(NCH):
                    dma(self.xsl("x", c, 0, T), dr["x0T"][c * 128:(c + 1) * 128, :])
                for n in ("nu_log_pm", "theta_log_pm", "gamma_log_pm", "D_pm",
                          "f_bo_pm", "f_g1_pm", "f_g2_pm", "cmask", "identity"):
                    dma(sb[n][:], dr[n][:])
                for c in range(NCH):
                    dma(sb["centroidT"][:, c * NE:(c + 1) * NE],
                        dr["centroidT"][c * 128:(c + 1) * 128, :])
                dma(sb["wdown_sb"][:].rearrange("p (c q) -> p c q", c=NCH),
                    dr["f_Wdown"][:].rearrange("(c p) q -> p c q", c=NCH))
                dma(sb["bdown_sb"][:], dr["f_bdown"][:])
                dma(sb["wgE_sb"][:].rearrange("p (c q) -> p c q", c=NCH),
                    dr["f_WgE"][:].rearrange("(c p) q -> p c q", c=NCH))
                e.wait_ge(p.sem("ld"), nld[0])

            @bl.vector
            def _(e):
                e.memset(sb["ones_col"][:], 1.0)
                e.memset(sb["ones_part"][:], 1.0)
                e.memset(sb["cm4"][:], -4.0)
                e.memset(sb["scr"][:, 8192:10240], 0.0)
                e.wait_ge(p.sem("ld"), nld[0])
                e.tensor_copy(sb["param8"][:, 0:1], sb["cmask"][:]
                              ).then_inc(p.sem("v"), 1)

            @bl.scalar
            def _(e):
                q = sb["param8"]
                e.wait_ge(p.sem("ld"), nld[0])
                e.activation(out=q[:, 0:8], in_=sb["nu_log_pm"][:], func=AF.Exp)
                e.activation(out=q[:, 8:16], in_=q[:, 0:8], func=AF.Exp, scale=-1.0)
                e.activation(out=q[:, 16:24], in_=sb["theta_log_pm"][:], func=AF.Exp)
                e.activation(out=sb["gamma"][:], in_=sb["gamma_log_pm"][:], func=AF.Exp
                             ).then_inc(p.sem("s"), 1)

            @bl.vector
            def _(e):
                q = sb["param8"]
                e.wait_ge(p.sem("s"), 1)
                theta = q[:, 16:24]
                p.tick(e.tensor_scalar(q[:, 24:32], theta, PI, None, OP.is_gt), e)
                p.tick(e.scalar_tensor_tensor(q[:, 32:40], q[:, 24:32], -2 * PI,
                                              theta, OP.mult, OP.add), e)
                p.tick(e.tensor_scalar(q[:, 40:48], q[:, 32:40], PI / 2, None,
                                       OP.add), e)
                p.tick(e.tensor_scalar(q[:, 48:56], q[:, 40:48], PI, None,
                                       OP.is_gt), e)
                e.scalar_tensor_tensor(q[:, 56:64], q[:, 48:56], -2 * PI,
                                       q[:, 40:48], OP.mult, OP.add
                                       ).then_inc(p.sem("v"), 1)

            @bl.scalar
            def _(e):
                q = sb["param8"]
                e.wait_ge(p.sem("v"), 2)
                e.activation(out=q[:, 64:72], in_=q[:, 32:40], func=AF.Sin)
                e.activation(out=q[:, 72:80], in_=q[:, 56:64], func=AF.Sin
                             ).then_inc(p.sem("s"), 1)

            @bl.vector
            def _(e):
                q = sb["param8"]
                e.wait_ge(p.sem("s"), 2)
                rad = q[:, 8:16]
                e.tensor_mul(sb["lam_im"][:], rad, q[:, 64:72])
                p.tick(e.tensor_mul(sb["lam_re"][:], rad, q[:, 72:80]), e)
                e.tensor_copy(sb["lp_re"][:, 0:8], sb["lam_re"][:])
                p.tick(e.tensor_copy(sb["lp_im"][:, 0:8], sb["lam_im"][:]), e)
                for k in range(1, 12):
                    pr = sb["lp_re"][:, (k - 1) * 8:k * 8]
                    pi_ = sb["lp_im"][:, (k - 1) * 8:k * 8]
                    nr = sb["lp_re"][:, k * 8:(k + 1) * 8]
                    ni = sb["lp_im"][:, k * 8:(k + 1) * 8]
                    e.tensor_mul(q[:, 80:88], pr, pr)
                    e.tensor_mul(q[:, 88:96], pi_, pi_)
                    p.tick(e.tensor_mul(ni, pr, pi_), e)
                    e.tensor_sub(nr, q[:, 80:88], q[:, 88:96])
                    p.tick(e.tensor_scalar(ni, ni, 2.0, None, OP.mult), e)
                e.tensor_scalar(sb["lp_im_neg"][:], sb["lp_im"][:], -1.0,
                                None, OP.mult)
                e.engine_nop()
        p.close()

    def phase1_ln(self, which):
        nc, sb = self.nc, self.sb
        name = f"ln{which}"
        pbS, pbB = self.pb[0], self.pb[1]
        p0r = sb["p0r"]
        with nc.Block(name) as bl:
            p = Phase(nc, name)
            self.phases[name] = p

            @bl.scalar
            def _(e):
                for tt in range(4):
                    if tt >= 2:
                        e.wait_ge(p.sem("mmsq"), (tt - 1) * (NCH - 1))
                    r = (tt % 2) * 4096
                    for c in range(NCH):
                        e.activation(out=sb["scr"][:, r + c * 512:r + c * 512 + 512],
                                     in_=self.xsl("x", c, tt * 512, 512),
                                     func=AF.Square).then_inc(p.sem("sq"), 1)
                    e.wait_ge(p.sem("vs1"), tt + 1)
                    e.activation(out=sb["bc_row"][0:1, :],
                                 in_=sb["scr"][0:1, 10240:10752], func=AF.Sqrt
                                 ).then_inc(p.sem("ssqrt"), 1)

            @bl.tensor
            def _(e):
                for tt in range(4):
                    r = (tt % 2) * 4096
                    if tt > 0:
                        e.wait_ge(p.sem("vstat"), tt)
                    for c in range(NCH):
                        e.matmul(pbS[0:1, 0:512], sb["ones_part"][:],
                                 self.xsl("x", c, tt * 512, 512),
                                 start=(c == 0), stop=(c == NCH - 1))
                    for c in range(NCH):
                        e.wait_ge(p.sem("sq"), tt * NCH + c + 1)
                        i = e.matmul(pbS[0:1, 512:1024], sb["ones_part"][:],
                                     sb["scr"][:, r + c * 512:r + c * 512 + 512],
                                     start=(c == 0), stop=(c == NCH - 1))
                        if c < NCH - 1:
                            i.then_inc(p.sem("mmsq"), 1)
                    i.then_inc(p.sem("mmstat"), 1)
                    e.wait_ge(p.sem("vrow"), tt + 1)
                    if tt > 0:
                        e.wait_ge(p.sem("vnorm"), tt)
                    e.matmul(pbB[:, 0:512], sb["ones_col"][:],
                             p0r[0:1, 0:512], start=True, stop=True)
                    i = e.matmul(pbB[:, 512:1024], sb["ones_col"][:],
                                 p0r[0:1, 512:1024], start=True, stop=True)
                    i.then_inc(p.sem("mmb"), 1)

            @bl.vector
            def _(e):
                eps = 1e-5
                m = p0r[0:1, 0:512]
                rr = p0r[0:1, 512:1024]
                a = sb["scr"][0:1, 10240:10752]
                t_ = sb["bc_row"][0:1, 0:512]
                for tt in range(4):
                    e.wait_ge(p.sem("mmstat"), tt + 1)
                    e.tensor_scalar(m, pbS[0:1, 0:512], 1.0 / DM, None, OP.mult)
                    p.tick(e.tensor_scalar(a, pbS[0:1, 512:1024], 1.0 / DM,
                                           None, OP.mult), e)
                    p.tick(e.tensor_mul(t_, m, m), e)
                    p.tick(e.tensor_sub(a, a, t_), e)
                    if which == 0:
                        e.tensor_copy(sb["scr"][0:1, 8192 + tt * 512:
                                                8192 + tt * 512 + 512], a)
                    e.tensor_scalar(a, a, eps, None, OP.add
                                    ).then_inc(p.sem("vs1"), 1)
                    e.wait_ge(p.sem("ssqrt"), tt + 1)
                    p.tick(e.reciprocal(rr, t_), e)
                    for _nw in range(2):
                        p.tick(e.tensor_mul(t_, rr, rr), e)
                        p.tick(e.tensor_mul(t_, t_, a), e)
                        p.tick(e.tensor_scalar(t_, t_, -0.5, 1.5,
                                               OP.mult, OP.add), e)
                        p.tick(e.tensor_mul(rr, rr, t_), e)
                    if which == 0:
                        v_ = sb["scr"][0:1, 8192 + tt * 512:8192 + tt * 512 + 512]
                        p.tick(e.tensor_mul(t_, rr, rr), e)
                        p.tick(e.tensor_mul(t_, t_, v_), e)
                        e.tensor_scalar(v_, t_, float(DM), None, OP.mult)
                    e.engine_nop().then_inc(p.sem("vrow"), 1)
                    e.wait_ge(p.sem("mmb"), tt + 1)
                    for c in range(NCH):
                        xt = self.xsl("x", c, tt * 512, 512)
                        xnt = self.xsl("xn", c, tt * 512, 512)
                        e.tensor_sub(xnt, xt, pbB[:, 0:512])
                        e.tensor_mul(xnt, xnt, pbB[:, 512:1024])
                    e.engine_nop().then_inc(p.sem("vnorm"), 1)
                    e.engine_nop().then_inc(p.sem("vstat"), 1)
        p.close()

    def phase2_routing(self):
        nc, sb = self.nc, self.sb
        pb0, pb1 = self.pb[0], self.pb[1]
        p0r = sb["p0r"]
        scr = sb["scr"]
        with nc.Block("rt") as bl:
            p = Phase(nc, "rt")
            self.phases["rt"] = p

            @bl.vector
            def _(e):
                e.tensor_mul(scr[:, 640:704], sb["centroidT"][:],
                             sb["centroidT"][:]).then_inc(p.sem("v0"), 1)

            @bl.tensor
            def _(e):
                e.wait_ge(p.sem("v0"), 1)
                for c in range(NCH):
                    i = e.matmul(pb0[0:1, 0:8], sb["ones_part"][:],
                                 scr[:, 640 + c * 8:648 + c * 8],
                                 start=(c == 0), stop=(c == NCH - 1))
                i.then_inc(p.sem("t0"), 1)
                for tt in range(16):
                    if tt > 0:
                        e.wait_ge(p.sem("vtr"), tt)
                    i = e.transpose(pb1[:, 0:128],
                                    scr[:, 8192 + tt * 128:8192 + (tt + 1) * 128],
                                    sb["identity"][:])
                    i.then_inc(p.sem("ttr"), 1)
                e.wait_ge(p.sem("v1"), 1)
                i = e.matmul(pb0[:, 512:520], sb["ones_col"][0:1, 0:128],
                             p0r[0:1, 0:8], start=True, stop=True)
                i.then_inc(p.sem("t1"), 1)
                e.wait_ge(p.sem("v2"), 1)
                for tt in range(16):
                    if tt > 1:
                        e.wait_ge(p.sem("vxc"), tt - 1)
                    xco = 8 + (tt % 2) * 512
                    for c in range(NCH):
                        i = e.matmul(pb0[:, xco:xco + 8],
                                     self.xsl("xn", c, tt * 128, 128),
                                     sb["centroidT"][:, c * NE:(c + 1) * NE],
                                     start=(c == 0), stop=(c == NCH - 1))
                    i.then_inc(p.sem("txc"), 1)
                for tt in range(16):
                    e.wait_ge(p.sem("vwe"), tt + 1)
                    if tt > 0:
                        e.wait_ge(p.sem("vwet"), tt)
                    i = e.transpose(pb1[:, 512:640], scr[:, 512:640],
                                    sb["identity"][:])
                    i.then_inc(p.sem("twe"), 1)

            @bl.sync
            def _(e):
                for tt in range(16):
                    e.wait_ge(p.sem("vwet"), tt + 1)
                    e.dma_start(out=self.dram["we_dram"][:, tt * 128:(tt + 1) * 128],
                                in_=sb["ev_b"][0:8, 0:128]
                                ).then_inc(p.sem("wedma"), 16)
                    e.wait_ge(p.sem("wedma"), (tt + 1) * 16)
                    e.nop().then_inc(p.sem("wed"), 1)

            @bl.vector
            def _(e):
                e.wait_ge(p.sem("t0"), 1)
                e.tensor_copy(p0r[0:1, 0:8], pb0[0:1, 0:8]
                              ).then_inc(p.sem("v1"), 1)
                for tt in range(16):
                    e.wait_ge(p.sem("ttr"), tt + 1)
                    p.tick(e.tensor_copy(sb["xn2_tm"][:, tt:tt + 1],
                                         pb1[:, 0:1]), e)
                    e.engine_nop().then_inc(p.sem("vtr"), 1)
                e.wait_ge(p.sem("t1"), 1)
                p.tick(e.tensor_copy(sb["c2b"][:], pb0[:, 512:520]), e)
                e.engine_nop().then_inc(p.sem("v2"), 1)
                for tt in range(16):
                    e.wait_ge(p.sem("txc"), tt + 1)
                    dcol = scr[:, tt * 8:(tt + 1) * 8]
                    xco = 8 + (tt % 2) * 512
                    p.tick(e.tensor_scalar(dcol, pb0[:, xco:xco + 8],
                                           -2.0, sb["xn2_tm"][:, tt:tt + 1],
                                           OP.mult, OP.add), e)
                    p.tick(e.tensor_add(dcol, dcol, sb["c2b"][:]), e)
                    p.tick(e.tensor_scalar(dcol, dcol, 1e-12, None, OP.max), e)
                    e.engine_nop().then_inc(p.sem("vxc"), 1)
                e.engine_nop().then_inc(p.sem("vd"), 1)
                e.wait_ge(p.sem("ssq"), 1)
                d2 = scr[:, 0:128]
                y0 = scr[:, 128:256]
                r0 = scr[:, 256:384]
                t_ = scr[:, 384:512]
                p.tick(e.reciprocal(r0, y0), e)
                for _ in range(2):
                    p.tick(e.tensor_mul(t_, d2, r0), e)
                    p.tick(e.tensor_add(y0, y0, t_), e)
                    p.tick(e.tensor_scalar(y0, y0, 0.5, None, OP.mult), e)
                    p.tick(e.reciprocal(r0, y0), e)
                p.tick(e.tensor_copy(d2, y0), e)
                for tt in range(16):
                    dcol = scr[:, tt * 8:(tt + 1) * 8]
                    m1 = sb["wrow"][:, 0:1]; m2 = sb["wrow"][:, 1:2]
                    w2 = sb["wrow"][:, 2:3]; dl = sb["wrow"][:, 3:4]
                    w1 = sb["wrow"][:, 4:5]
                    msk1 = scr[:, 384:392]; d2b = scr[:, 392:400]
                    msk2 = scr[:, 400:408]
                    p.tick(e.reduce_max(m1, dcol, AX.X), e)
                    p.tick(e.tensor_scalar(msk1, dcol, m1, None, OP.is_equal), e)
                    p.tick(e.scalar_tensor_tensor(d2b, msk1, -1e30, dcol,
                                                  OP.mult, OP.add), e)
                    p.tick(e.reduce_max(m2, d2b, AX.X), e)
                    p.tick(e.tensor_scalar(msk2, d2b, m2, None, OP.is_equal), e)
                    e.tensor_sub(dl, m2, m1).then_inc(p.sem("vsg"), 1)
                    e.wait_ge(p.sem("ssg"), tt + 1)
                    p.tick(e.tensor_scalar(w1, w2, -1.0, 1.0, OP.mult, OP.add), e)
                    we_t = scr[:, 512:640]
                    if tt == 0:
                        e.memset(we_t[:, 8:128], 0.0)
                    p.tick(e.tensor_scalar(we_t[:, 0:8], msk1, w1, None,
                                           OP.mult), e)
                    e.scalar_tensor_tensor(we_t[:, 0:8], msk2, w2, we_t[:, 0:8],
                                           OP.mult, OP.add
                                           ).then_inc(p.sem("vwe"), 1)
                    e.wait_ge(p.sem("twe"), tt + 1)
                    if tt > 0:
                        e.wait_ge(p.sem("wed"), tt)
                    e.tensor_copy(sb["ev_b"][0:8, 0:128], pb1[0:8, 512:640]
                                  ).then_inc(p.sem("vwet"), 1)

            @bl.scalar
            def _(e):
                e.wait_ge(p.sem("vd"), 1)
                e.activation(out=sb["scr"][:, 128:256], in_=sb["scr"][:, 0:128],
                             func=AF.Sqrt).then_inc(p.sem("ssq"), 1)
                for tt in range(16):
                    e.wait_ge(p.sem("vsg"), tt + 1)
                    e.activation(out=sb["wrow"][:, 2:3], in_=sb["wrow"][:, 3:4],
                                 func=AF.Sigmoid).then_inc(p.sem("ssg"), 1)
        p.close()

    def phase3_scan(self):
        nc, sb, dr = self.nc, self.sb, self.dram
        pbA, pbB, pbC = self.pb[0], self.pb[1], self.pb[2]
        scr, wbuf = sb["scr"], sb["wbuf"]

        def xs(pcl, pl):
            o = (pcl * 2 + pl) * 2048
            return scr[:, o:o + 2048]

        def ping(pl):
            return wbuf[:, pl * 2048:(pl + 1) * 2048]

        with nc.Block("scan") as bl:
            p = Phase(nc, "scan")
            self.phases["scan"] = p

            @bl.sync
            def _(e):
                for w in range(4):
                    if w > 0:
                        e.wait_ge(p.sem("cxev"), w * 32)
                    for pcl in range(2):
                        pc = w * 2 + pcl
                        for pl, nm in enumerate(("B_reT", "B_imT")):
                            dst = wbuf[:, (pcl * 2 + pl) * 1024:
                                       (pcl * 2 + pl) * 1024 + 1024]
                            e.dma_start(
                                out=dst.rearrange("p (c q) -> p c q", c=NCH),
                                in_=dr[nm][:, pc * 128:(pc + 1) * 128]
                                .rearrange("(c p) q -> p c q", c=NCH)
                            ).then_inc(p.sem("bw"), 16)
                    e.wait_ge(p.sem("carry"), w + 1)
                    if w > 0:
                        e.wait_ge(p.sem("cc"), w)
                    e.dma_start(out=dr["cc_carry_in"][:], in_=sb["carry"][:]
                                ).then_inc(p.sem("cst"), 16)
                    e.wait_ge(p.sem("corr"), w + 1)
                    if SCAN_WAVE_DUMP:
                        e.dma_start(out=dr["dbg_xs_waves"][:, w * 8192:(w + 1) * 8192],
                                    in_=scr[:, 0:8192]).then_inc(p.sem("dmp"), 16)
                    for pcl in range(2):
                        pc = w * 2 + pcl
                        for pl, nm in enumerate(("C_reT", "C_imT")):
                            dst = wbuf[:, (pcl * 2 + pl) * 1024:
                                       (pcl * 2 + pl) * 1024 + 1024]
                            e.dma_start(out=dst,
                                        in_=dr[nm][pc * 128:(pc + 1) * 128, :]
                                        ).then_inc(p.sem("cw"), 16)
                e.wait_ge(p.sem("cst"), 64)
                if SCAN_WAVE_DUMP:
                    e.wait_ge(p.sem("cxev"), 128)
                    d = self.dram["dbg_small"]
                    e.dma_start(out=d[:, 0:96], in_=sb["lp_re"][:]
                                ).then_inc(p.sem("dsm"), 16)
                    e.dma_start(out=d[:, 96:192], in_=sb["lp_im"][:]
                                ).then_inc(p.sem("dsm"), 16)
                    e.dma_start(out=d[:, 192:200], in_=sb["carry_in"][:]
                                ).then_inc(p.sem("dsm"), 16)
                    e.dma_start(out=d[:, 200:204], in_=sb["c0"][:]
                                ).then_inc(p.sem("dsm"), 16)
                    e.dma_start(out=d[:, 204:212], in_=sb["gamma"][:]
                                ).then_inc(p.sem("dsm"), 16)
                    e.dma_start(out=d[:, 212:220], in_=sb["lam_re"][:]
                                ).then_inc(p.sem("dsm"), 16)
                    e.dma_start(out=d[:, 220:228], in_=sb["lam_im"][:]
                                ).then_inc(p.sem("dsm"), 16)
                    e.wait_ge(p.sem("dsm"), 112)

            @bl.gpsimd
            def _(e):
                for w in range(4):
                    e.wait_ge(p.sem("cst"), (w + 1) * 16)
                    e.collective_compute(
                        "AllGather", OP.bypass, replica_groups=RG,
                        ins=[dr["cc_carry_in"]], outs=[dr["cc_carry_out"]]
                    ).then_inc(p.sem("cc"), 1)
                    e.wait_ge(p.sem("cc"), w + 1)
                    e.dma_start(out=sb["carry_in"][:, 0:4],
                                in_=dr["cc_carry_out"][0:128, :]
                                ).then_inc(p.sem("cld"), 16)
                e.wait_ge(p.sem("cld"), 64)

            @bl.tensor
            def _(e):
                gb = 0
                gc = 0
                for w in range(4):
                    e.wait_ge(p.sem("bw"), (w + 1) * 64)
                    for pcl in range(2):
                        for pl in range(2):
                            lhs_blk = wbuf[:, (pcl * 2 + pl) * 1024:
                                           (pcl * 2 + pl) * 1024 + 1024]
                            pbank = pbA if pl == 0 else pbB
                            for tt in range(4):
                                if gb >= 2:
                                    e.wait_ge(p.sem("buev"), gb - 1)
                                for c in range(NCH):
                                    i = e.matmul(
                                        pbank[:, (tt % 2) * 512:(tt % 2) * 512 + 512],
                                        lhs_blk[:, c * 128:(c + 1) * 128],
                                        self.xsl("xn", c, tt * 512, 512),
                                        start=(c == 0), stop=(c == NCH - 1))
                                i.then_inc(p.sem("bu"), 1)
                                gb += 1
                    e.wait_ge(p.sem("cneg"), w + 1)
                    for hc in range(NCH):
                        for tt in range(4):
                            if gc >= 2:
                                e.wait_ge(p.sem("cxev"), gc - 1)
                            first = True
                            for pcl in range(2):
                                for pl in range(2):
                                    lhs = wbuf[:, (pcl * 2 + pl) * 1024 + hc * 128:
                                               (pcl * 2 + pl) * 1024 + hc * 128 + 128]
                                    i = e.matmul(
                                        pbC[:, (tt % 2) * 512:(tt % 2) * 512 + 512],
                                        lhs,
                                        xs(pcl, pl)[:, tt * 512:tt * 512 + 512],
                                        start=first,
                                        stop=(pcl == 1 and pl == 1))
                                    first = False
                            i.then_inc(p.sem("cx"), 1)
                            gc += 1

            @bl.vector
            def _(e):
                nbu = 0
                ncx = 0
                for w in range(4):
                    if SCAN_WAVE_DUMP and w > 0:
                        e.wait_ge(p.sem("dmp"), w * 16)
                    for pcl in range(2):
                        pc = w * 2 + pcl
                        for pl in range(2):
                            pbank = pbA if pl == 0 else pbB
                            for tt in range(4):
                                nbu += 1
                                e.wait_ge(p.sem("bu"), nbu)
                                e.tensor_scalar(
                                    xs(pcl, pl)[:, tt * 512:tt * 512 + 512],
                                    pbank[:, (tt % 2) * 512:(tt % 2) * 512 + 512],
                                    sb["gamma"][:, pc:pc + 1], None, OP.mult
                                ).then_inc(p.sem("buev"), 1)
                    for pcl in range(2):
                        pc = w * 2 + pcl
                        bufs = [(xs(pcl, 0), xs(pcl, 1)), (ping(0), ping(1))]
                        cur = 0
                        for k in range(12):
                            ir, ii = bufs[cur]
                            orr, oi = bufs[1 - cur]
                            if k == 11:
                                e.tensor_copy(orr, ir)
                                e.tensor_copy(oi, ii)
                                cur = 1 - cur
                                continue
                            d = 1 << k
                            cre = sb["lp_re"][:, k * 8 + pc:k * 8 + pc + 1]
                            cim = sb["lp_im"][:, k * 8 + pc:k * 8 + pc + 1]
                            cimn = sb["lp_im_neg"][:, k * 8 + pc:k * 8 + pc + 1]
                            e.tensor_copy(orr[:, 0:d], ir[:, 0:d])
                            e.tensor_copy(oi[:, 0:d], ii[:, 0:d])
                            e.tensor_scalar(orr[:, d:2048], ir[:, 0:2048 - d],
                                            cre, None, OP.mult)
                            e.tensor_scalar(oi[:, d:2048], ii[:, 0:2048 - d],
                                            cre, None, OP.mult)
                            e.engine_nop()
                            e.scalar_tensor_tensor(orr[:, d:2048], ii[:, 0:2048 - d],
                                                   cimn, orr[:, d:2048],
                                                   OP.mult, OP.add)
                            e.scalar_tensor_tensor(oi[:, d:2048], ir[:, 0:2048 - d],
                                                   cim, oi[:, d:2048],
                                                   OP.mult, OP.add)
                            e.engine_nop()
                            e.tensor_add(orr[:, d:2048], orr[:, d:2048],
                                         ir[:, d:2048])
                            e.tensor_add(oi[:, d:2048], oi[:, d:2048],
                                         ii[:, d:2048])
                            e.engine_nop()
                            cur = 1 - cur
                        e.tensor_copy(sb["carry"][:, pcl * 2:pcl * 2 + 1],
                                      xs(pcl, 0)[:, 2047:2048])
                        i_cp = e.tensor_copy(sb["carry"][:, pcl * 2 + 1:pcl * 2 + 2],
                                             xs(pcl, 1)[:, 2047:2048])
                        e.engine_nop()
                    i_cp.then_inc(p.sem("carry"), 1)
                    e.wait_ge(p.sem("cld"), (w + 1) * 16)
                    p.tick(e.tensor_scalar(sb["c0"][:], sb["carry_in"][:, 0:4],
                                           sb["cmask"][:], None, OP.mult), e)
                    p.tick(e.tensor_scalar(sb["c0n"][:], sb["c0"][:], -1.0,
                                           None, OP.mult), e)
                    for pcl in range(2):
                        pc = w * 2 + pcl
                        powre, powim = ping(0), ping(1)
                        self._build_powtab(e, powre, powim, pc, p)
                        c0re = sb["c0"][:, pcl * 2:pcl * 2 + 1]
                        c0im = sb["c0"][:, pcl * 2 + 1:pcl * 2 + 2]
                        c0imn = sb["c0n"][:, pcl * 2 + 1:pcl * 2 + 2]
                        e.scalar_tensor_tensor(xs(pcl, 0), powre, c0re,
                                               xs(pcl, 0), OP.mult, OP.add)
                        e.scalar_tensor_tensor(xs(pcl, 1), powim, c0re,
                                               xs(pcl, 1), OP.mult, OP.add)
                        e.engine_nop()
                        e.scalar_tensor_tensor(xs(pcl, 0), powim, c0imn,
                                               xs(pcl, 0), OP.mult, OP.add)
                        e.scalar_tensor_tensor(xs(pcl, 1), powre, c0im,
                                               xs(pcl, 1), OP.mult, OP.add)
                        e.engine_nop()
                    e.engine_nop().then_inc(p.sem("corr"), 1)
                    e.wait_ge(p.sem("cw"), (w + 1) * 64)
                    for pcl in range(2):
                        sl = wbuf[:, (pcl * 2 + 1) * 1024:(pcl * 2 + 1) * 1024 + 1024]
                        e.tensor_scalar(sl, sl, -1.0, None, OP.mult)
                    e.engine_nop().then_inc(p.sem("cneg"), 1)
                    for hc in range(NCH):
                        for tt in range(4):
                            ncx += 1
                            e.wait_ge(p.sem("cx"), ncx)
                            xt = self.xsl("x", hc, tt * 512, 512)
                            if w == 0:
                                e.scalar_tensor_tensor(
                                    xt, self.xsl("xn", hc, tt * 512, 512),
                                    sb["D_pm"][:, hc:hc + 1], xt, OP.mult, OP.add)
                            e.tensor_add(xt, xt,
                                         pbC[:, (tt % 2) * 512:(tt % 2) * 512 + 512]
                                         ).then_inc(p.sem("cxev"), 1)
        p.close()

    def _build_powtab(self, e, powre, powim, pc, p):
        sb = self.sb
        e.tensor_copy(powre[:, 0:1], sb["lam_re"][:, pc:pc + 1])
        p.tick(e.tensor_copy(powim[:, 0:1], sb["lam_im"][:, pc:pc + 1]), e)
        for k in range(7):
            w = 1 << k
            cre = sb["lp_re"][:, k * 8 + pc:k * 8 + pc + 1]
            cim = sb["lp_im"][:, k * 8 + pc:k * 8 + pc + 1]
            cimn = sb["lp_im_neg"][:, k * 8 + pc:k * 8 + pc + 1]
            e.tensor_scalar(powre[:, w:2 * w], powre[:, 0:w], cre, None, OP.mult)
            p.tick(e.tensor_scalar(powim[:, w:2 * w], powim[:, 0:w], cre,
                                   None, OP.mult), e)
            e.scalar_tensor_tensor(powre[:, w:2 * w], powim[:, 0:w], cimn,
                                   powre[:, w:2 * w], OP.mult, OP.add)
            p.tick(e.scalar_tensor_tensor(powim[:, w:2 * w], powre[:, 0:w], cim,
                                          powim[:, w:2 * w], OP.mult, OP.add), e)
        c128re = sb["lp_re"][:, 7 * 8 + pc:7 * 8 + pc + 1]
        c128im = sb["lp_im"][:, 7 * 8 + pc:7 * 8 + pc + 1]
        c128imn = sb["lp_im_neg"][:, 7 * 8 + pc:7 * 8 + pc + 1]
        e.tensor_copy(sb["cj_re"][:], c128re)
        e.tensor_copy(sb["cj_im"][:], c128im)
        p.tick(e.tensor_copy(sb["cj_imn"][:], c128imn), e)
        for j in range(1, 16):
            o = j * 128
            e.tensor_scalar(powre[:, o:o + 128], powre[:, 0:128],
                            sb["cj_re"][:], None, OP.mult)
            p.tick(e.tensor_scalar(powim[:, o:o + 128], powim[:, 0:128],
                                   sb["cj_re"][:], None, OP.mult), e)
            e.scalar_tensor_tensor(powre[:, o:o + 128], powim[:, 0:128],
                                   sb["cj_imn"][:], powre[:, o:o + 128],
                                   OP.mult, OP.add)
            p.tick(e.scalar_tensor_tensor(powim[:, o:o + 128], powre[:, 0:128],
                                          sb["cj_im"][:], powim[:, o:o + 128],
                                          OP.mult, OP.add), e)
            if j < 15:
                e.tensor_scalar(sb["cj2_re"][:], sb["cj_re"][:], c128re, None,
                                OP.mult)
                p.tick(e.tensor_scalar(sb["cj2_im"][:], sb["cj_im"][:], c128re,
                                       None, OP.mult), e)
                e.scalar_tensor_tensor(sb["cj2_re"][:], sb["cj_im"][:], c128imn,
                                       sb["cj2_re"][:], OP.mult, OP.add)
                p.tick(e.scalar_tensor_tensor(sb["cj2_im"][:], sb["cj_re"][:],
                                              c128im, sb["cj2_im"][:],
                                              OP.mult, OP.add), e)
                e.tensor_copy(sb["cj_re"][:], sb["cj2_re"][:])
                p.tick(e.tensor_copy(sb["cj_im"][:], sb["cj2_im"][:]), e)
                p.tick(e.tensor_scalar(sb["cj_imn"][:], sb["cj_im"][:], -1.0,
                                       None, OP.mult), e)

    def phase4_moe(self):
        nc, sb, dr = self.nc, self.sb, self.dram
        pb0, pb1, pb2, pb3 = self.pb
        scr, wbuf = sb["scr"], sb["wbuf"]
        with nc.Block("moe") as bl:
            p = Phase(nc, "moe")
            self.phases["moe"] = p

            @bl.sync
            def _(e):
                it = 0
                for ei in range(NE):
                    e.wait_ge(p.sem("b2u"), ei)
                    e.dma_start(out=sb["b2_t"][:], in_=dr["moe_b2_pm"][ei]
                                ).then_inc(p.sem("b2l"), 16)
                    e.dma_start(out=sb["bias_moe"][:, 0:NHT],
                                in_=dr["moe_bg_pm"][ei]).then_inc(p.sem("b2l"), 16)
                    e.dma_start(out=sb["bias_moe"][:, NHT:2 * NHT],
                                in_=dr["moe_b1_pm"][ei]).then_inc(p.sem("b2l"), 16)
                    for tt in range(4):
                        e.wait_ge(p.sem("webc"), it)
                        e.dma_start(out=sb["bc_row"][0:1, :],
                                    in_=dr["we_dram"][ei:ei + 1,
                                                      tt * 512:(tt + 1) * 512]
                                    ).then_inc(p.sem("bcl"), 16)
                        for ht in range(NHT):
                            hs, hn = HT_OFF[ht], HT_SIZES[ht]
                            if ht < 2:
                                e.wait_ge(p.sem("mm2h"), it * 16)
                            else:
                                e.wait_ge(p.sem("hev"), it * NHT + ht - 1)
                            buf = (ht % 2) * 2048
                            e.dma_start(
                                out=wbuf[:, buf:buf + 1024]
                                .rearrange("p (c q) -> p c q", c=NCH)[:, :, 0:hn],
                                in_=dr["moe_Wg"][ei, :, hs:hs + hn]
                                .rearrange("(c p) q -> p c q", c=NCH)
                                ).then_inc(p.sem("wgl"), 16)
                            e.dma_start(
                                out=wbuf[:, buf + 1024:buf + 2048]
                                .rearrange("p (c q) -> p c q", c=NCH)[:, :, 0:hn],
                                in_=dr["moe_W1"][ei, :, hs:hs + hn]
                                .rearrange("(c p) q -> p c q", c=NCH)
                                ).then_inc(p.sem("wgl"), 16)
                        for oc in range(NE):
                            for hf in range(2):
                                s_idx = oc * 2 + hf
                                if s_idx < 2:
                                    e.wait_ge(p.sem("mm1"), (it + 1) * NHT)
                                else:
                                    e.wait_ge(p.sem("mm2h"), it * 16 + s_idx - 1)
                                buf2 = (s_idx % 2) * 1408
                                if hf == 0:
                                    e.dma_start(
                                        out=wbuf[:, buf2:buf2 + 1408]
                                        .rearrange("p (b q) -> p b q", b=11),
                                        in_=dr["moe_W2"][ei, 0:1408,
                                                         oc * 128:(oc + 1) * 128]
                                        .rearrange("(b p) q -> p b q", p=128)
                                        ).then_inc(p.sem("w2l"), 16)
                                else:
                                    e.dma_start(
                                        out=wbuf[:, buf2:buf2 + 1280]
                                        .rearrange("p (b q) -> p b q", b=10),
                                        in_=dr["moe_W2"][ei, 1408:2688,
                                                         oc * 128:(oc + 1) * 128]
                                        .rearrange("(b p) q -> p b q", p=128)
                                        ).then_inc(p.sem("w2l"), 16)
                                    e.dma_start(
                                        out=wbuf[0:48, buf2 + 1280:buf2 + 1408]
                                        .rearrange("p (b q) -> p b q", b=1),
                                        in_=dr["moe_W2"][ei, 2688:2736,
                                                         oc * 128:(oc + 1) * 128]
                                        .rearrange("(b p) q -> p b q", p=48)
                                        ).then_inc(p.sem("w2l"), 16)
                        it += 1
                e.wait_ge(p.sem("oev"), 32 * 8)

            @bl.tensor
            def _(e):
                it = 0
                for ei in range(NE):
                    for tt in range(4):
                        e.wait_ge(p.sem("bcl"), (it + 1) * 16)
                        if it > 0:
                            e.wait_ge(p.sem("oev"), it * 8)
                        e.matmul(pb3[:, 0:512], sb["ones_col"][:],
                                 sb["bc_row"][0:1, :], start=True, stop=True
                                 ).then_inc(p.sem("webc"), 1)
                        nw2 = 0
                        for ht in range(NHT):
                            hn = HT_SIZES[ht]
                            e.wait_ge(p.sem("wgl"), (it * NHT + ht + 1) * 32)
                            if ht >= 2:
                                e.wait_ge(p.sem("hev"), it * NHT + ht - 1)
                            half = (ht % 2) * 512
                            for c in range(NCH):
                                i = e.matmul(pb0[0:hn, half:half + 512],
                                             wbuf[:, (ht % 2) * 2048 + c * 128:
                                                  (ht % 2) * 2048 + c * 128 + hn],
                                             self.xsl("xn", c, tt * 512, 512),
                                             start=(c == 0), stop=(c == NCH - 1))
                            i.then_inc(p.sem("mmg"), 1)
                            for c in range(NCH):
                                i = e.matmul(pb1[0:hn, half:half + 512],
                                             wbuf[:, (ht % 2) * 2048 + 1024 + c * 128:
                                                  (ht % 2) * 2048 + 1024 + c * 128 + hn],
                                             self.xsl("xn", c, tt * 512, 512),
                                             start=(c == 0), stop=(c == NCH - 1))
                            i.then_inc(p.sem("mm1"), 1)
                        for oc in range(NE):
                            if oc == 0:
                                e.wait_ge(p.sem("hev"), (it + 1) * NHT)
                            half = (oc % 2) * 512
                            if oc >= 2:
                                e.wait_ge(p.sem("oev"), it * 8 + oc - 1)
                            for hf in range(2):
                                s_idx = oc * 2 + hf
                                e.wait_ge(p.sem("w2l"),
                                          self._w2l_target(it, s_idx))
                                buf2 = (s_idx % 2) * 1408
                                for hb in range(11):
                                    ht = hf * 11 + hb
                                    hn = HT_SIZES[ht]
                                    last = e.matmul(
                                        pb2[:, half:half + 512],
                                        wbuf[0:hn, buf2 + hb * 128:buf2 + hb * 128 + 128],
                                        scr[0:hn, ht * 512:ht * 512 + 512],
                                        start=(ht == 0), stop=(ht == NHT - 1))
                                last.then_inc(p.sem("mm2h"), 1)
                        it += 1

            @bl.scalar
            def _(e):
                it = 0
                for ei in range(NE):
                    e.wait_ge(p.sem("b2l"), (ei + 1) * 48)
                    for tt in range(4):
                        for ht in range(NHT):
                            hn = HT_SIZES[ht]
                            e.wait_ge(p.sem("mmg"), it * NHT + ht + 1)
                            if ht >= 2:
                                e.wait_ge(p.sem("hev"), it * NHT + ht - 1)
                            half = (ht % 2) * 512
                            ev = sb["ev_a"] if ht % 2 == 0 else sb["ev_b"]
                            e.activation(out=ev[0:hn, :],
                                         in_=pb0[0:hn, half:half + 512],
                                         func=AF.Silu,
                                         bias=sb["bias_moe"][0:hn, ht:ht + 1]
                                         ).then_inc(p.sem("ssil"), 1)
                        it += 1

            @bl.vector
            def _(e):
                it = 0
                for ei in range(NE):
                    e.wait_ge(p.sem("b2l"), (ei + 1) * 48)
                    for tt in range(4):
                        for ht in range(NHT):
                            hn = HT_SIZES[ht]
                            e.wait_ge(p.sem("ssil"), it * NHT + ht + 1)
                            e.wait_ge(p.sem("mm1"), it * NHT + ht + 1)
                            half = (ht % 2) * 512
                            ev = sb["ev_a"] if ht % 2 == 0 else sb["ev_b"]
                            e.scalar_tensor_tensor(
                                scr[0:hn, ht * 512:ht * 512 + 512],
                                pb1[0:hn, half:half + 512],
                                sb["bias_moe"][0:hn, NHT + ht:NHT + ht + 1],
                                ev[0:hn, :], OP.add, OP.mult
                            ).then_inc(p.sem("hev"), 1)
                        e.wait_ge(p.sem("webc"), it + 1)
                        for oc in range(NE):
                            e.wait_ge(p.sem("mm2h"), it * 16 + (oc + 1) * 2)
                            half = (oc % 2) * 512
                            e.tensor_scalar(sb["ev_a"][:],
                                            pb2[:, half:half + 512],
                                            sb["b2_t"][:, oc:oc + 1], None, OP.add)
                            e.engine_nop()
                            e.tensor_mul(sb["ev_a"][:], sb["ev_a"][:], pb3[:, 0:512])
                            e.engine_nop()
                            xt = self.xsl("x", oc, tt * 512, 512)
                            e.tensor_add(xt, xt, sb["ev_a"][:]
                                         ).then_inc(p.sem("oev"), 1)
                        it += 1
                    e.engine_nop().then_inc(p.sem("b2u"), 1)
        p.close()

    def _w2l_target(self, it, s_idx):
        per_iter = 16 * 16 + 8 * 16
        base = it * per_iter
        cnt = 0
        for s in range(s_idx + 1):
            cnt += 16
            if s % 2 == 1:
                cnt += 16
        return base + cnt

    def phase6_kv(self):
        nc, sb, dr = self.nc, self.sb, self.dram
        pb0 = self.pb[0]
        scr, wbuf = sb["scr"], sb["wbuf"]
        with nc.Block("kv") as bl:
            p = Phase(nc, "kv")
            self.phases["kv"] = p

            @bl.sync
            def _(e):
                for oc in range(2):
                    e.dma_start(
                        out=wbuf[:, (oc % 2) * 1024:(oc % 2) * 1024 + 1024]
                        .rearrange("p (c q) -> p c q", c=NCH),
                        in_=dr["attn_kernel"][1, :, oc * 128:(oc + 1) * 128]
                        .rearrange("(c p) q -> p c q", c=NCH)
                        ).then_inc(p.sem("wkl"), 16)
                for oc in range(NE):
                    e.wait_ge(p.sem("kev"), (oc + 1) * 4)
                    e.dma_start(out=dr["cc_K_in"][oc * 128:(oc + 1) * 128, :],
                                in_=scr[:, 0:2048]).then_inc(p.sem("kst"), 16)
                    e.wait_ge(p.sem("kst"), (oc + 1) * 16)
                    e.nop().then_inc(p.sem("kdone"), 1)
                    if oc + 2 < NE:
                        e.wait_ge(p.sem("kev"), (oc + 1) * 4)
                        e.dma_start(
                            out=wbuf[:, ((oc + 2) % 2) * 1024:
                                     ((oc + 2) % 2) * 1024 + 1024]
                            .rearrange("p (c q) -> p c q", c=NCH),
                            in_=dr["attn_kernel"][1, :, (oc + 2) * 128:(oc + 3) * 128]
                            .rearrange("(c p) q -> p c q", c=NCH)
                            ).then_inc(p.sem("wkl"), 16)
                for oh in range(2):
                    if oh > 0:
                        e.wait_ge(p.sem("vev"), oh * 16)
                    e.dma_start(
                        out=wbuf[:, 0:4096].rearrange("p (c q) -> p c q", c=NCH),
                        in_=dr["attn_kernel"][2, :, oh * 512:(oh + 1) * 512]
                        .rearrange("(c p) q -> p c q", c=NCH)
                        ).then_inc(p.sem("wvl"), 16)
                    for tt in range(16):
                        e.wait_ge(p.sem("vev"), oh * 16 + tt + 1)
                        e.dma_start(
                            out=dr["cc_V_in"][tt * 128:(tt + 1) * 128,
                                              oh * 512:(oh + 1) * 512],
                            in_=scr[:, 2048 + (tt % 2) * 512:
                                    2048 + (tt % 2) * 512 + 512]
                            ).then_inc(p.sem("vst"), 16)
                        e.wait_ge(p.sem("vst"), (oh * 16 + tt + 1) * 16)
                        e.nop().then_inc(p.sem("vdone"), 1)

            @bl.tensor
            def _(e):
                for oc in range(NE):
                    e.wait_ge(p.sem("wkl"), (max(oc, 1) + 1) * 16)
                    for tt in range(4):
                        g = oc * 4 + tt
                        if g >= 2:
                            e.wait_ge(p.sem("kev"), g - 1)
                        for c in range(NCH):
                            i = e.matmul(pb0[:, (tt % 2) * 512:(tt % 2) * 512 + 512],
                                         wbuf[:, (oc % 2) * 1024 + c * 128:
                                              (oc % 2) * 1024 + c * 128 + 128],
                                         self.xsl("xn", c, tt * 512, 512),
                                         start=(c == 0), stop=(c == NCH - 1))
                        i.then_inc(p.sem("kmm"), 1)
                for oh in range(2):
                    e.wait_ge(p.sem("wvl"), (oh + 1) * 16)
                    for tt in range(16):
                        g = oh * 16 + tt
                        if g >= 2:
                            e.wait_ge(p.sem("vev"), g - 1)
                        for c in range(NCH):
                            i = e.matmul(pb0[:, (tt % 2) * 512:(tt % 2) * 512 + 512],
                                         self.xsl("xn", c, tt * 128, 128),
                                         wbuf[:, c * 512:(c + 1) * 512],
                                         start=(c == 0), stop=(c == NCH - 1))
                        i.then_inc(p.sem("vmm"), 1)

            @bl.vector
            def _(e):
                for oc in range(NE):
                    for tt in range(4):
                        e.wait_ge(p.sem("kmm"), oc * 4 + tt + 1)
                        if oc > 0 and tt == 0:
                            e.wait_ge(p.sem("kdone"), oc)
                        e.tensor_copy(scr[:, tt * 512:(tt + 1) * 512],
                                      pb0[:, (tt % 2) * 512:(tt % 2) * 512 + 512]
                                      ).then_inc(p.sem("kev"), 1)
                for oh in range(2):
                    for tt in range(16):
                        g = oh * 16 + tt
                        e.wait_ge(p.sem("vmm"), g + 1)
                        if g >= 2:
                            e.wait_ge(p.sem("vdone"), g - 1)
                        e.tensor_copy(scr[:, 2048 + (tt % 2) * 512:
                                          2048 + (tt % 2) * 512 + 512],
                                      pb0[:, (tt % 2) * 512:(tt % 2) * 512 + 512]
                                      ).then_inc(p.sem("vev"), 1)

            @bl.gpsimd
            def _(e):
                e.wait_ge(p.sem("kdone"), NE)
                e.collective_compute(
                    "AllGather", OP.bypass, replica_groups=RG,
                    ins=[dr["cc_K_in"]], outs=[dr["cc_K_out"]]
                ).then_inc(p.sem("ccdone"), 1)
                e.wait_ge(p.sem("vdone"), 32)
                e.collective_compute(
                    "AllGather", OP.bypass, replica_groups=RG,
                    ins=[dr["cc_V_in"]], outs=[dr["cc_V_out"]]
                ).then_inc(p.sem("ccdone"), 1)
                e.wait_ge(p.sem("ccdone"), 2)
        p.close()

    def phase7_attn(self):
        nc, sb, dr = self.nc, self.sb, self.dram
        pb0, pb1, pb2, pb3 = self.pb
        scr, wbuf = sb["scr"], sb["wbuf"]
        with nc.Block("att") as bl:
            p = Phase(nc, "att")
            self.phases["att"] = p

            @bl.sync
            def _(e):
                hi = 0
                for wv in range(2):
                    for h in range(16):
                        if hi >= 1:
                            e.wait_ge(p.sem("smm"), hi * 32)
                        for r in range(2):
                            e.dma_start(
                                out=scr[0:64, r * 2048:r * 2048 + 2048],
                                in_=dr["cc_K_out"][r * DM + h * 64:
                                                   r * DM + h * 64 + 64, :]
                                ).then_inc(p.sem("ktl"), 16)
                        e.wait_ge(p.sem("v65i"), 3)
                        if hi >= 1:
                            e.wait_ge(p.sem("pv"), hi * 32)
                        for r in range(2):
                            e.dma_start(
                                out=scr[:, 4096 + r * 16 * 65:
                                        4096 + (r + 1) * 16 * 65]
                                .rearrange("p (k d) -> p k d", d=65)[:, :, 0:64],
                                in_=dr["cc_V_out"][r * T:(r + 1) * T,
                                                   h * 64:h * 64 + 64]
                                .rearrange("(k p) d -> p k d", p=128)
                                ).then_inc(p.sem("vl"), 16)
                        if hi >= 1:
                            e.wait_ge(p.sem("qmm"), hi)
                        e.dma_start(
                            out=scr[:, 7200:7712]
                            .rearrange("p (c q) -> p c q", c=NCH),
                            in_=dr["attn_kernel"][0, :, h * 64:h * 64 + 64]
                            .rearrange("(c p) q -> p c q", c=NCH)
                            ).then_inc(p.sem("wql"), 16)
                        hi += 1

            @bl.vector
            def _(e):
                e.memset(scr[:, 4096:6176]
                         .rearrange("p (k d) -> p k d", d=65)[:, :, 64:65], 1.0
                         ).then_inc(p.sem("v65i"), 1)
                e.memset(scr[64:128, 0:4096], 0.0).then_inc(p.sem("v65i"), 1)
                e.memset(scr[64:128, 6176:7200], 0.0
                         ).then_inc(p.sem("v65i"), 1)
                hi = 0
                for wv in range(2):
                    for h in range(16):
                        e.wait_ge(p.sem("qmm"), hi + 1)
                        qh = scr[0:64, 6176:7200]
                        e.tensor_copy(qh, pb3[0:64, 0:1024]
                                      ).then_inc(p.sem("qev"), 1)
                        e.wait_ge(p.sem("pv"), (hi + 1) * 32)
                        e.reciprocal(sb["p0r"][0:1, 0:1024], pb2[64:65, 0:1024]
                                     ).then_inc(p.sem("rrow"), 1)
                        e.wait_ge(p.sem("rmm"), hi + 1)
                        base = (h % 2) * 64
                        recb = wbuf[base:base + 64, 2048:3072]
                        e.tensor_copy(recb, pb3[0:64, 0:1024])
                        e.engine_nop()
                        ev = wbuf[base:base + 64, 3072:4096]
                        e.tensor_mul(ev, pb2[0:64, 0:1024], recb)
                        e.engine_nop()
                        xt = sb["x"][base:base + 64,
                                     (h // 2) * T + wv * 1024:
                                     (h // 2) * T + wv * 1024 + 1024]
                        e.tensor_add(xt, xt, ev)
                        e.engine_nop().then_inc(p.sem("xup"), 1)
                        hi += 1

            @bl.scalar
            def _(e):
                hi = 0
                for wv in range(2):
                    for h in range(16):
                        for kt in range(32):
                            e.wait_ge(p.sem("smm"), hi * 32 + kt + 1)
                            sbank = pb0 if kt % 2 == 0 else pb1
                            dst = wbuf[:, (kt % 2) * 1024:(kt % 2) * 1024 + 1024]
                            e.activation(out=dst, in_=sbank[:, 0:1024],
                                         func=AF.Exp, scale=0.125, bias=sb["cm4"][:]
                                         ).then_inc(p.sem("sexp"), 1)
                        hi += 1

            @bl.tensor
            def _(e):
                hi = 0
                for wv in range(2):
                    for h in range(16):
                        e.wait_ge(p.sem("wql"), (hi + 1) * 16)
                        if hi > 0:
                            e.wait_ge(p.sem("xup"), hi)
                        for c in range(NCH):
                            i = self.mm(e, pb3[0:64, 0:1024],
                                        scr[:, 7200 + c * 64:7200 + (c + 1) * 64],
                                        self.xsl("xn", c, wv * 1024, 1024),
                                        start=(c == 0), stop=(c == NCH - 1))
                        i.then_inc(p.sem("qmm"), 1)
                        e.wait_ge(p.sem("ktl"), (hi + 1) * 32)
                        e.wait_ge(p.sem("vl"), (hi + 1) * 32)
                        e.wait_ge(p.sem("qev"), hi + 1)
                        if hi == 0:
                            e.wait_ge(p.sem("v65i"), 3)
                        for kt in range(33):
                            if kt < 32:
                                sbank = pb0 if kt % 2 == 0 else pb1
                                if kt >= 2:
                                    e.wait_ge(p.sem("sexp"), hi * 32 + kt - 1)
                                i = self.mm(e, sbank[:, 0:1024],
                                            scr[0:128, kt * 128:kt * 128 + 128],
                                            scr[0:128, 6176:7200],
                                            start=True, stop=True)
                                i.then_inc(p.sem("smm"), 1)
                            if kt >= 1:
                                kp = kt - 1
                                e.wait_ge(p.sem("sexp"), hi * 32 + kt)
                                v65 = scr[:, 4096 + kp * 65:4096 + kp * 65 + 65]
                                i = self.mm(e, pb2[0:65, 0:1024], v65,
                                            wbuf[:, (kp % 2) * 1024:
                                                 (kp % 2) * 1024 + 1024],
                                            start=(kp == 0), stop=(kp == 31))
                                i.then_inc(p.sem("pv"), 1)

                        e.wait_ge(p.sem("rrow"), hi + 1)
                        i = self.mm(e, pb3[0:64, 0:1024], sb["ones_col"][0:1, 0:64],
                                    sb["p0r"][0:1, 0:1024], start=True, stop=True)
                        i.then_inc(p.sem("rmm"), 1)
                        hi += 1
        p.close()

    def phase8_ffnA(self):
        nc, sb, dr = self.nc, self.sb, self.dram
        pb0, pb1, pb2, pb3 = self.pb
        scr, wbuf = sb["scr"], sb["wbuf"]
        p0r = sb["p0r"]
        with nc.Block("ffA") as bl:
            p = Phase(nc, "ffA")
            self.phases["ffA"] = p

            @bl.sync
            def _(e):
                e.dma_start(out=scr[0:65, 8704:10752], in_=dr["f_Wup"][:]
                            ).then_inc(p.sem("wupl"), 16)
                for tt in range(4):
                    e.wait_ge(p.sem("mmb"), tt + 1)
                    e.dma_start(out=p0r[0:1, 0:1024], in_=dr["f_bq"][:]
                                ).then_inc(p.sem("bql"), 16)
                    for c in range(NCH):
                        if tt > 0 and c < 2:
                            e.wait_ge(p.sem("vout"), tt * 4)
                        if tt > 0 or c >= 2:
                            e.wait_ge(p.sem("qcmm"), tt * NCH + c - 1)
                        e.dma_start(
                            out=scr[:, 4608 + (c % 2) * 1024:
                                    4608 + (c % 2) * 1024 + 1024],
                            in_=dr["f_Wq"][c * 128:(c + 1) * 128, :]
                            ).then_inc(p.sem("wq"), 16)
                    for jc in range(NE):
                        if tt * 8 + jc >= 1:
                            e.wait_ge(p.sem("womm"), tt * 8 + jc)
                        e.dma_start(
                            out=sb["ev_a"][:].rearrange("p (c q) -> p c q", c=4),
                            in_=dr["f_Wo"][0:512, jc * 128:(jc + 1) * 128]
                            .rearrange("(c p) q -> p c q", c=4)
                            ).then_inc(p.sem("wol"), 16)
                        e.dma_start(
                            out=sb["ev_b"][:].rearrange("p (c q) -> p c q", c=4),
                            in_=dr["f_Wo"][512:1024, jc * 128:(jc + 1) * 128]
                            .rearrange("(c p) q -> p c q", c=4)
                            ).then_inc(p.sem("wol"), 16)

            @bl.scalar
            def _(e):
                for tt in range(4):
                    if tt > 0:
                        e.wait_ge(p.sem("womm"), tt * 8)
                    for c in range(NCH):
                        e.activation(out=scr[:, c * 512:(c + 1) * 512],
                                     in_=self.xsl("xn", c, tt * 512, 512),
                                     func=AF.Square).then_inc(p.sem("sq"), 1)
                    e.wait_ge(p.sem("vs1"), tt + 1)
                    e.activation(out=p0r[0:1, 0:512], in_=p0r[0:1, 512:1024],
                                 func=AF.Sqrt).then_inc(p.sem("ssq"), 1)
                    for q4 in range(4):
                        e.wait_ge(p.sem("vatt"), tt * 4 + q4 + 1)
                        e.activation(out=scr[:, 11008:11264],
                                     in_=scr[:, 10752:11008], func=AF.Exp
                                     ).then_inc(p.sem("sex"), 1)

            @bl.tensor
            def _(e):
                e.wait_ge(p.sem("wupl"), 16)
                for tt in range(4):
                    if tt > 0:
                        e.wait_ge(p.sem("qev"), tt * 4)
                    for c in range(NCH):
                        e.wait_ge(p.sem("sq"), tt * NCH + c + 1)
                        i = e.matmul(pb0[0:1, 0:512], sb["ones_part"][:],
                                     scr[:, c * 512:(c + 1) * 512],
                                     start=(c == 0), stop=(c == NCH - 1))
                    i.then_inc(p.sem("mmstat"), 1)
                    e.wait_ge(p.sem("vrow"), tt + 1)
                    i = e.matmul(pb0[:, 512:1024], sb["ones_col"][:],
                                 p0r[0:1, 0:512], start=True, stop=True)
                    i.then_inc(p.sem("mmb"), 1)
                    e.wait_ge(p.sem("hready"), tt + 1)
                    for c in range(NCH):
                        e.wait_ge(p.sem("wq"), (tt * NCH + max(c, 1) + 1) * 16)
                        wqc = scr[:, 4608 + (c % 2) * 1024:
                                  4608 + (c % 2) * 1024 + 1024]
                        for q4 in range(4):
                            i = self.mm(e, self.pb[q4][:, 0:1024],
                                        scr[:, c * 512 + q4 * 128:
                                            c * 512 + q4 * 128 + 128],
                                        wqc, start=(c == 0), stop=False)
                        i.then_inc(p.sem("qcmm"), 1)
                    e.wait_ge(p.sem("bql"), (tt + 1) * 16)
                    for q4 in range(4):
                        i = self.mm(e, self.pb[q4][:, 0:1024],
                                    sb["ones_col"][:], p0r[0:1, 0:1024],
                                    start=False, stop=True)
                    i.then_inc(p.sem("qmm"), 1)
                    e.wait_ge(p.sem("qev"), (tt + 1) * 4)
                    for c in range(NCH):
                        i = e.matmul(pb2[0:64, 0:512],
                                     sb["wdown_sb"][:, c * 64:(c + 1) * 64],
                                     self.xsl("xn", c, tt * 512, 512),
                                     start=(c == 0), stop=(c == NCH - 1))
                    i.then_inc(p.sem("hdmm"), 1)
                    def emit_T(q4):
                        e.wait_ge(p.sem("vout"), tt * 4 + q4 + 1)
                        for c in range(NCH):
                            g = tt * 32 + q4 * 8 + c
                            if g >= 2:
                                e.wait_ge(p.sem("tev"), g - 1)
                            i = e.transpose(pb0[:, (c % 2) * 512:(c % 2) * 512 + 128],
                                            scr[:, 7680 + c * 128:7680 + (c + 1) * 128],
                                            sb["identity"][:])
                            i.then_inc(p.sem("tmm"), 1)
                    for q4 in range(4):
                        e.wait_ge(p.sem("hdev"), tt + 1)
                        if tt * 4 + q4 >= 1:
                            e.wait_ge(p.sem("kvev"), tt * 4 + q4)
                        i = self.mm(e, pb2[:, 0:1024],
                                    scr[0:65, 4096 + q4 * 128:4096 + q4 * 128 + 128],
                                    scr[0:65, 8704:9728], start=True, stop=True)
                        i = self.mm(e, pb3[:, 0:1024],
                                    scr[0:65, 4096 + q4 * 128:4096 + q4 * 128 + 128],
                                    scr[0:65, 9728:10752], start=True, stop=True)
                        i.then_inc(p.sem("kvmm"), 1)
                        if q4 >= 1:
                            emit_T(q4 - 1)
                    emit_T(3)
                    for jc in range(NE):
                        e.wait_ge(p.sem("wol"), (tt * 8 + jc + 1) * 32)
                        e.wait_ge(p.sem("ofm"), tt * 4 + 4)
                        g = tt * 8 + jc
                        if g >= 2:
                            e.wait_ge(p.sem("woev"), g - 1)
                        for c2 in range(4):
                            e.matmul(pb1[:, (jc % 2) * 512:(jc % 2) * 512 + 512],
                                     sb["ev_a"][:, c2 * 128:(c2 + 1) * 128],
                                     scr[:, c2 * 512:c2 * 512 + 512],
                                     start=(c2 == 0), stop=False)
                        for c2 in range(4):
                            i = e.matmul(pb1[:, (jc % 2) * 512:(jc % 2) * 512 + 512],
                                         sb["ev_b"][:, c2 * 128:(c2 + 1) * 128],
                                         scr[:, (4 + c2) * 512:(4 + c2) * 512 + 512],
                                         start=False, stop=(c2 == 3))
                        i.then_inc(p.sem("womm"), 1)

            @bl.vector
            def _(e):
                for tt in range(4):
                    e.wait_ge(p.sem("mmstat"), tt + 1)
                    a = p0r[0:1, 512:1024]
                    rr = p0r[0:1, 0:512]
                    tmp = sb["bc_row"][0:1, 0:512]
                    e.tensor_scalar(a, pb0[0:1, 0:512], 1.0 / DM, 1e-6,
                                    OP.mult, OP.add).then_inc(p.sem("vs1"), 1)
                    e.wait_ge(p.sem("ssq"), tt + 1)
                    p.tick(e.reciprocal(rr, rr), e)
                    for _nw in range(2):
                        p.tick(e.tensor_mul(tmp, rr, rr), e)
                        p.tick(e.tensor_mul(tmp, tmp, a), e)
                        p.tick(e.tensor_scalar(tmp, tmp, -0.5, 1.5,
                                               OP.mult, OP.add), e)
                        p.tick(e.tensor_mul(rr, rr, tmp), e)
                    e.engine_nop().then_inc(p.sem("vrow"), 1)
                    e.wait_ge(p.sem("mmb"), tt + 1)
                    for c in range(NCH):
                        ht = scr[:, c * 512:(c + 1) * 512]
                        e.tensor_mul(ht, self.xsl("xn", c, tt * 512, 512),
                                     pb0[:, 512:1024])
                        e.engine_nop()
                        e.tensor_scalar(ht, ht, sb["f_g1_pm"][:, c:c + 1],
                                        None, OP.mult)
                    e.engine_nop().then_inc(p.sem("hready"), 1)
                    for q4 in range(4):
                        e.wait_ge(p.sem("qmm"), tt + 1)
                        qt = wbuf[:, q4 * 1024:(q4 + 1) * 1024]
                        e.tensor_copy(qt, self.pb[q4][:, 0:1024])
                        e.engine_nop().then_inc(p.sem("qev"), 1)
                    e.wait_ge(p.sem("hdmm"), tt + 1)
                    e.tensor_scalar(scr[0:64, 4096:4608], pb2[0:64, 0:512],
                                    sb["bdown_sb"][:], None, OP.add)
                    e.memset(scr[64:65, 4096:4608], 1.0
                             ).then_inc(p.sem("hdev"), 1)
                    for q4 in range(4):
                        e.wait_ge(p.sem("kvmm"), tt * 4 + q4 + 1)
                        kv = scr[:, 4608:6656]
                        e.tensor_copy(kv[:, 0:1024], pb2[:, 0:1024])
                        e.tensor_copy(kv[:, 1024:2048], pb3[:, 0:1024])
                        e.engine_nop().then_inc(p.sem("kvev"), 1)
                        qt = wbuf[:, q4 * 1024:(q4 + 1) * 1024]
                        att = scr[:, 10752:11008]
                        att2 = scr[:, 11008:11264]
                        for h in range(16):
                            q3 = qt[:, h * 64:h * 64 + 64].rearrange(
                                "p (g d) -> p g d", g=1).broadcast_to([128, 16, 64])
                            k3 = kv.rearrange("p (g d2) -> p g d2", g=16)[:, :, 0:64]
                            prod = scr[:, 6656:7680].rearrange(
                                "p (g d) -> p g d", g=16)
                            e.tensor_tensor(prod, q3, k3, OP.mult)
                            e.engine_nop()
                            i_red = e.tensor_reduce(att[:, h * 16:(h + 1) * 16],
                                                    prod, AX.X, OP.add)
                            if h == 15:
                                p.tick(i_red, e)
                            else:
                                e.engine_nop()
                        att3 = att.rearrange("p (h g) -> p h g", h=16)
                        p.tick(e.tensor_scalar(att, att, 0.125, None, OP.mult), e)
                        p.tick(e.tensor_reduce(sb["att_mx"][:], att3, AX.X,
                                               OP.max), e)
                        mx3 = sb["att_mx"][:].rearrange("p (h g) -> p h g", g=1
                                                        ).broadcast_to([128, 16, 16])
                        e.tensor_tensor(att3, att3, mx3, OP.subtract
                                        ).then_inc(p.sem("vatt"), 1)
                        e.wait_ge(p.sem("sex"), tt * 4 + q4 + 1)
                        at23 = att2.rearrange("p (h g) -> p h g", h=16)
                        p.tick(e.tensor_reduce(sb["att_s"][:], at23, AX.X,
                                               OP.add), e)
                        p.tick(e.reciprocal(sb["att_s"][:], sb["att_s"][:]), e)
                        s3 = sb["att_s"][:].rearrange("p (h g) -> p h g", g=1
                                                      ).broadcast_to([128, 16, 16])
                        p.tick(e.tensor_tensor(at23, at23, s3, OP.mult), e)
                        for h in range(16):
                            a3 = att2[:, h * 16:(h + 1) * 16].rearrange(
                                "p (d g) -> p d g", d=1).broadcast_to([128, 64, 16])
                            v3 = kv.rearrange("p (g d2) -> p g d2",
                                              g=16)[:, :, 64:128]
                            v3t = v3.transpose([0, 2, 1])
                            prod2 = scr[:, 6656:7680].rearrange(
                                "p (d g) -> p d g", d=64)
                            e.tensor_tensor(prod2, a3, v3t, OP.mult)
                            e.engine_nop()
                            i_rd = e.tensor_reduce(
                                scr[:, 7680 + h * 64:7680 + h * 64 + 64],
                                prod2, AX.X, OP.add)
                            e.engine_nop()
                        i_rd.then_inc(p.sem("vout"), 1)
                        for c in range(NCH):
                            g = tt * 32 + q4 * 8 + c
                            e.wait_ge(p.sem("tmm"), g + 1)
                            e.tensor_copy(scr[:, c * 512 + q4 * 128:
                                              c * 512 + q4 * 128 + 128],
                                          pb0[:, (c % 2) * 512:(c % 2) * 512 + 128]
                                          ).then_inc(p.sem("tev"), 1)
                        e.engine_nop().then_inc(p.sem("ofm"), 1)
                    for jc in range(NE):
                        e.wait_ge(p.sem("womm"), tt * 8 + jc + 1)
                        xnt = self.xsl("xn", jc, tt * 512, 512)
                        e.scalar_tensor_tensor(
                            xnt, pb1[:, (jc % 2) * 512:(jc % 2) * 512 + 512],
                            sb["f_bo_pm"][:, jc:jc + 1], xnt, OP.add, OP.add)
                        e.engine_nop()
                        xt = self.xsl("x", jc, tt * 512, 512)
                        e.tensor_add(xt, xt, xnt)
                        e.engine_nop().then_inc(p.sem("woev"), 1)
        p.close()

    def phase9_ffnB(self):
        nc, sb, dr = self.nc, self.sb, self.dram
        pb0, pb1, pb2, pb3 = self.pb
        scr, wbuf = sb["scr"], sb["wbuf"]
        p0r = sb["p0r"]
        with nc.Block("ffB") as bl:
            p = Phase(nc, "ffB")
            self.phases["ffB"] = p

            @bl.sync
            def _(e):
                for tt in range(4):
                    e.wait_ge(p.sem("mmb"), tt + 1)
                    e.dma_start(out=p0r[0:1, 0:8], in_=dr["f_bgE"][:]
                                ).then_inc(p.sem("bgl"), 16)
                    e.wait_ge(p.sem("gev"), tt + 1)
                    e.dma_start(out=dr["gate_dram"][:],
                                in_=sb["ev_b"][0:8, 0:512]
                                ).then_inc(p.sem("gst"), 16)
                    e.wait_ge(p.sem("gst"), (tt + 1) * 16)
                    e.nop().then_inc(p.sem("gstd"), 1)
                    for ei in range(NE):
                        it = tt * 8 + ei
                        e.wait_ge(p.sem("gbc"), it)
                        e.dma_start(out=sb["bc_row"][0:1, :],
                                    in_=dr["gate_dram"][ei:ei + 1, :]
                                    ).then_inc(p.sem("bcl"), 16)
                        for qr in range(4):
                            g = it * 4 + qr
                            if g >= 2:
                                e.wait_ge(p.sem("wemm"), (g - 1) * 2)
                            e.dma_start(
                                out=wbuf[:, (qr % 2) * 2048:(qr % 2) * 2048 + 2048]
                                .rearrange("p (c q) -> p c q", c=NCH),
                                in_=dr["f_We"][ei, :, qr * 256:(qr + 1) * 256]
                                .rearrange("(c p) q -> p c q", c=NCH)
                                ).then_inc(p.sem("wel"), 16)
                e.wait_ge(p.sem("xev"), 4 * 8 * 8)

            @bl.scalar
            def _(e):
                for tt in range(4):
                    if tt > 0:
                        e.wait_ge(p.sem("wemm"), tt * 64)
                    for c in range(NCH):
                        e.activation(out=scr[:, c * 512:(c + 1) * 512],
                                     in_=self.xsl("xn", c, tt * 512, 512),
                                     func=AF.Square).then_inc(p.sem("sq"), 1)
                    e.wait_ge(p.sem("vs1"), tt + 1)
                    e.activation(out=p0r[0:1, 0:512], in_=p0r[0:1, 512:1024],
                                 func=AF.Sqrt).then_inc(p.sem("ssq"), 1)
                    e.wait_ge(p.sem("vg1"), tt + 1)
                    e.activation(out=sb["gate_tm"][:, 32:64],
                                 in_=sb["gate_tm"][:, 0:32], func=AF.Exp
                                 ).then_inc(p.sem("sge"), 1)

            @bl.tensor
            def _(e):
                for tt in range(4):
                    for c in range(NCH):
                        e.wait_ge(p.sem("sq"), tt * NCH + c + 1)
                        i = e.matmul(pb0[0:1, 0:512], sb["ones_part"][:],
                                     scr[:, c * 512:(c + 1) * 512],
                                     start=(c == 0), stop=(c == NCH - 1))
                    i.then_inc(p.sem("mmstat"), 1)
                    e.wait_ge(p.sem("vrow"), tt + 1)
                    i = e.matmul(pb0[:, 512:1024], sb["ones_col"][:],
                                 p0r[0:1, 0:512], start=True, stop=True)
                    i.then_inc(p.sem("mmb"), 1)
                    e.wait_ge(p.sem("h2ready"), tt + 1)
                    e.wait_ge(p.sem("bgl"), (tt + 1) * 16)
                    if tt > 0:
                        e.wait_ge(p.sem("xev"), tt * 64)
                    for q4 in range(4):
                        for c in range(NCH):
                            e.matmul(pb1[:, q4 * 8:(q4 + 1) * 8],
                                     scr[:, c * 512 + q4 * 128:c * 512 + q4 * 128 + 128],
                                     sb["wgE_sb"][:, c * 8:(c + 1) * 8],
                                     start=(c == 0), stop=False)
                        i = e.matmul(pb1[:, q4 * 8:(q4 + 1) * 8],
                                     sb["ones_col"][0:1, 0:128],
                                     p0r[0:1, 0:8], start=False, stop=True)
                    i.then_inc(p.sem("gmm"), 1)
                    for q4 in range(4):
                        e.wait_ge(p.sem("vg2"), tt * 4 + q4 + 1)
                        if tt * 4 + q4 >= 1:
                            e.wait_ge(p.sem("gtev"), tt * 4 + q4)
                        i = e.transpose(pb2[:, 0:128], sb["ev_a"][:, 0:128],
                                        sb["identity"][:])
                        i.then_inc(p.sem("gtr"), 1)
                    e.wait_ge(p.sem("gcp"), tt + 1)
                    for ei in range(NE):
                        it = tt * 8 + ei
                        e.wait_ge(p.sem("bcl"), (it + 1) * 16)
                        if it > 0:
                            e.wait_ge(p.sem("gbu"), it)
                        i = e.matmul(pb3[:, 0:512], sb["ones_col"][:],
                                     sb["bc_row"][0:1, :], start=True, stop=True)
                        i.then_inc(p.sem("gbc"), 1)
                        for oc in range(NE):
                            g = it * 8 + oc
                            qr = oc // 2
                            oc2 = oc % 2
                            if oc2 == 0:
                                tq = it * 4 + qr
                                e.wait_ge(p.sem("wel"), (max(tq, 1) + 1) * 16)
                            if g >= 2:
                                e.wait_ge(p.sem("xev"), g - 1)
                            for c in range(NCH):
                                i = e.matmul(
                                    pb1[:, (oc % 2) * 512:(oc % 2) * 512 + 512],
                                    wbuf[:, (qr % 2) * 2048 + c * 256 + oc2 * 128:
                                         (qr % 2) * 2048 + c * 256 + oc2 * 128 + 128],
                                    scr[:, c * 512:(c + 1) * 512],
                                    start=(c == 0), stop=(c == NCH - 1))
                            i.then_inc(p.sem("wemm"), 1)

            @bl.vector
            def _(e):
                for tt in range(4):
                    e.wait_ge(p.sem("mmstat"), tt + 1)
                    a = p0r[0:1, 512:1024]
                    rr = p0r[0:1, 0:512]
                    tmp = scr[0:1, 4096:4608]
                    e.tensor_scalar(a, pb0[0:1, 0:512], 1.0 / DM, 1e-6,
                                    OP.mult, OP.add).then_inc(p.sem("vs1"), 1)
                    e.wait_ge(p.sem("ssq"), tt + 1)
                    p.tick(e.reciprocal(rr, rr), e)
                    for _nw in range(2):
                        p.tick(e.tensor_mul(tmp, rr, rr), e)
                        p.tick(e.tensor_mul(tmp, tmp, a), e)
                        p.tick(e.tensor_scalar(tmp, tmp, -0.5, 1.5,
                                               OP.mult, OP.add), e)
                        p.tick(e.tensor_mul(rr, rr, tmp), e)
                    e.engine_nop().then_inc(p.sem("vrow"), 1)
                    e.wait_ge(p.sem("mmb"), tt + 1)
                    for c in range(NCH):
                        h2 = scr[:, c * 512:(c + 1) * 512]
                        e.tensor_mul(h2, self.xsl("xn", c, tt * 512, 512),
                                     pb0[:, 512:1024])
                        e.engine_nop()
                        e.tensor_scalar(h2, h2, sb["f_g2_pm"][:, c:c + 1],
                                        None, OP.mult)
                    e.engine_nop().then_inc(p.sem("h2ready"), 1)
                    e.wait_ge(p.sem("gmm"), tt + 1)
                    g = sb["gate_tm"][:, 0:32]
                    g2 = sb["gate_tm"][:, 32:64]
                    e.tensor_copy(g, pb1[:, 0:32]).then_inc(p.sem("gcp"), 1)
                    g3 = g.rearrange("p (f g8) -> p f g8", f=4)
                    p.tick(e.tensor_reduce(sb["att_mx"][:, 0:4], g3, AX.X,
                                           OP.max), e)
                    mx3 = sb["att_mx"][:, 0:4].rearrange(
                        "p (f g8) -> p f g8", g8=1).broadcast_to([128, 4, 8])
                    e.tensor_tensor(g3, g3, mx3, OP.subtract
                                    ).then_inc(p.sem("vg1"), 1)
                    e.wait_ge(p.sem("sge"), tt + 1)
                    g23 = g2.rearrange("p (f g8) -> p f g8", f=4)
                    p.tick(e.tensor_reduce(sb["att_s"][:, 0:4], g23, AX.X,
                                           OP.add), e)
                    p.tick(e.reciprocal(sb["att_s"][:, 0:4],
                                        sb["att_s"][:, 0:4]), e)
                    s3 = sb["att_s"][:, 0:4].rearrange(
                        "p (f g8) -> p f g8", g8=1).broadcast_to([128, 4, 8])
                    p.tick(e.tensor_tensor(g23, g23, s3, OP.mult), e)
                    for q4 in range(4):
                        e.memset(sb["ev_a"][:, 0:128], 0.0)
                        e.engine_nop()
                        e.tensor_copy(sb["ev_a"][:, 0:8], g2[:, q4 * 8:(q4 + 1) * 8]
                                      ).then_inc(p.sem("vg2"), 1)
                        e.wait_ge(p.sem("gtr"), tt * 4 + q4 + 1)
                        e.tensor_copy(sb["ev_b"][0:8, q4 * 128:(q4 + 1) * 128],
                                      pb2[0:8, 0:128])
                        e.engine_nop().then_inc(p.sem("gtev"), 1)
                    e.engine_nop().then_inc(p.sem("gev"), 1)
                    for ei in range(NE):
                        it = tt * 8 + ei
                        e.wait_ge(p.sem("gbc"), it + 1)
                        e.wait_ge(p.sem("gstd"), tt + 1)
                        e.tensor_copy(sb["ev_b"][:], pb3[:, 0:512])
                        e.engine_nop().then_inc(p.sem("gbu"), 1)
                        for oc in range(NE):
                            gx = it * 8 + oc
                            e.wait_ge(p.sem("wemm"), gx + 1)
                            e.tensor_mul(sb["ev_a"][:],
                                         pb1[:, (oc % 2) * 512:(oc % 2) * 512 + 512],
                                         sb["ev_b"][:])
                            e.engine_nop()
                            xt = self.xsl("x", oc, tt * 512, 512)
                            e.tensor_add(xt, xt, sb["ev_a"][:]
                                         ).then_inc(p.sem("xev"), 1)
        p.close()

    def phase10_store(self):
        nc, sb, dr = self.nc, self.sb, self.dram
        with nc.Block("st") as bl:
            p = Phase(nc, "st")

            @bl.sync
            def _(e):
                for c in range(NCH):
                    e.dma_start(out=dr["outT"][c * 128:(c + 1) * 128, :],
                                in_=self.xsl("x", c, 0, T)
                                ).then_inc(p.sem("st"), 16)
                e.wait_ge(p.sem("st"), NCH * 16)
        p.close()

def _hpad(b):
    b = np.asarray(b, np.float32)
    out = np.zeros((NE, NHT * 128), dtype=np.float32)
    out[:, :HMOE] = b
    return np.ascontiguousarray(out.reshape(NE, NHT, 128).transpose(0, 2, 1))


def prep_inputs(inputs, core):
    b, half = core // 2, core % 2
    t0 = half * T
    asc = np.ascontiguousarray
    f32 = np.float32

    def pm(v):
        return asc(np.asarray(v, f32).reshape(NCH, 128).T)

    m = {
        "x0T": asc(np.asarray(inputs["L0_attn_inputs"][b, t0:t0 + T], f32).T),
        "nu_log_pm": pm(inputs["L0_attn_nu_log"]),
        "theta_log_pm": pm(inputs["L0_attn_theta_log"]),
        "gamma_log_pm": pm(inputs["L0_attn_gamma_log"]),
        "D_pm": pm(inputs["L0_attn_D"]),
        "B_reT": asc(np.asarray(inputs["L0_attn_B_re"], f32).T),
        "B_imT": asc(np.asarray(inputs["L0_attn_B_im"], f32).T),
        "C_reT": asc(np.asarray(inputs["L0_attn_C_re"], f32).T),
        "C_imT": asc(np.asarray(inputs["L0_attn_C_im"], f32).T),
        "centroidT": asc(np.asarray(inputs["L0_ffn_centroid"], f32).T),
        "moe_Wg": np.asarray(inputs["L0_ffn_Wg"], f32),
        "moe_W1": np.asarray(inputs["L0_ffn_W1"], f32),
        "moe_W2": np.asarray(inputs["L0_ffn_W2"], f32),
        "moe_bg_pm": _hpad(inputs["L0_ffn_bg"]),
        "moe_b1_pm": _hpad(inputs["L0_ffn_b1"]),
        "moe_b2_pm": np.asarray(inputs["L0_ffn_b2"], f32)
        .reshape(NE, NCH, 128).transpose(0, 2, 1).copy(),
        "attn_kernel": np.asarray(inputs["L1_attn_kernel"], f32),
        "f_Wq": np.asarray(inputs["L1_ffn_Wq"], f32),
        "f_bq": np.asarray(inputs["L1_ffn_bq"], f32).reshape(1, DM),
        "f_Wdown": np.asarray(inputs["L1_ffn_Wdown"], f32),
        "f_bdown": np.asarray(inputs["L1_ffn_bdown"], f32).reshape(64, 1),
        "f_Wup": asc(np.concatenate(
            [np.asarray(inputs["L1_ffn_Wup"], f32),
             np.asarray(inputs["L1_ffn_bup"], f32).reshape(1, 2 * DM)], 0)),
        "f_Wo": np.asarray(inputs["L1_ffn_Wo"], f32),
        "f_bo_pm": pm(inputs["L1_ffn_bo"]),
        "f_g1_pm": pm(inputs["L1_ffn_gamma1"]),
        "f_g2_pm": pm(inputs["L1_ffn_gamma2"]),
        "f_WgE": np.asarray(inputs["L1_ffn_Wg"], f32),
        "f_bgE": np.asarray(inputs["L1_ffn_bg"], f32).reshape(1, NE),
        "f_We": np.asarray(inputs["L1_ffn_We"], f32),
        "identity": np.eye(128, dtype=f32),
        "cmask": np.full((128, 1), float(half), dtype=f32),
    }
    return {k: np.ascontiguousarray(v) for k, v in m.items()}


def kernel(**inputs):
    nc = Builder().build()
    in_maps = [prep_inputs(inputs, c) for c in range(8)]
    res = run_bass_kernel_spmd(nc, in_maps, core_ids=list(range(8)))
    out = np.zeros((4, 4096, 1024), dtype=np.float32)
    for c in range(8):
        b, half = c // 2, c % 2
        out[b, half * T:(half + 1) * T] = res.results[c]["outT"].T
    return out
